# version 03
# baseline (speedup 1.0000x reference)
import contextlib

import numpy as np

import concourse.bass as bass
import concourse.mybir as mybir
from concourse.bass_utils import run_bass_kernel_spmd

F32 = mybir.dt.float32
FP8 = mybir.dt.float8e4
F16 = mybir.dt.float16
AF = mybir.ActivationFunctionType
OP = mybir.AluOpType

B, S, D, F, H = 4, 4096, 1024, 3072, 4096
T = 2048
KD, KF, KH = D // 128, F // 128, H // 128
TC1, NCH1 = 512, 4
TC3, NCH3 = 256, 8
INV_D = 1.0 / D


class Sem:
    def __init__(self, nc, st, name):
        self.h = st.enter_context(nc.semaphore(name))
        self.v = 0

    def bump(self, n=1):
        self.v += n
        return self.v


def build_nc():
    nc = bass.Bass("TRN2", target_bir_lowering=False, debug=False)

    xT = nc.dram_tensor("xT", [D, T], F32, kind="ExternalInput").ap()
    wgT = nc.dram_tensor("wgT", [D, D], F32, kind="ExternalInput").ap()
    wvT = nc.dram_tensor("wvT", [D, D], F32, kind="ExternalInput").ap()
    wdT = nc.dram_tensor("wdT", [D, D], F32, kind="ExternalInput").ap()
    wgateT = nc.dram_tensor("wgateT", [D, F], F32, kind="ExternalInput").ap()
    wupT = nc.dram_tensor("wupT", [D, F], F32, kind="ExternalInput").ap()
    woutT = nc.dram_tensor("woutT", [F, D], F32, kind="ExternalInput").ap()
    w1T = nc.dram_tensor("w1T", [D, H], F32, kind="ExternalInput").ap()
    w2T = nc.dram_tensor("w2T", [H, D], F32, kind="ExternalInput").ap()
    bg2 = nc.dram_tensor("bg2", [128, KD], F32, kind="ExternalInput").ap()
    bv2 = nc.dram_tensor("bv2", [128, KD], F32, kind="ExternalInput").ap()
    bd2 = nc.dram_tensor("bd2", [128, KD], F32, kind="ExternalInput").ap()
    b12 = nc.dram_tensor("b12", [128, KH], F32, kind="ExternalInput").ap()
    b22 = nc.dram_tensor("b22", [128, KD], F32, kind="ExternalInput").ap()
    wmix2 = nc.dram_tensor("wmix2", [128, KD], F32, kind="ExternalInput").ap()
    wffn2 = nc.dram_tensor("wffn2", [128, KD], F32, kind="ExternalInput").ap()
    selp = nc.dram_tensor("selp", [128, 1], F32, kind="ExternalInput").ap()

    outT = nc.dram_tensor("outT", [D, T], F32, kind="ExternalOutput").ap()

    xn_h = nc.dram_tensor("xn_h", [D, T], F32).ap()
    xs_h = nc.dram_tensor("xs_h", [D, T], F32).ap()
    a_h = nc.dram_tensor("a_h", [D, T], F32).ap()
    cc_in = nc.dram_tensor("cc_in", [128, KD], F32)
    cc_out = nc.dram_tensor("cc_out", [2, 128, KD], F32)

    def fm(ap_):
        return ap_.rearrange("(k p) t -> p k t", p=128)

    xTv, xnv, xsv, av, outv = fm(xT), fm(xn_h), fm(xs_h), fm(a_h), fm(outT)
    wgv = wgT.rearrange("(k p) m -> p k m", p=128)
    wvv = wvT.rearrange("(k p) m -> p k m", p=128)
    wdv = wdT.rearrange("(k p) m -> p k m", p=128)
    wgatev = wgateT.rearrange("(k p) m -> p k m", p=128)
    wupv = wupT.rearrange("(k p) m -> p k m", p=128)
    w1v = w1T.rearrange("(k p) m -> p k m", p=128)
    w2v = w2T.rearrange("(k p) m -> p k m", p=128)
    woutv = woutT.rearrange("(k p) m -> k p m", p=128)

    P = []

    def step(eng, body=None, waits=(), incs=()):
        waits = [(s.h, v) for (s, v) in waits]
        incs_ = [(s.h, n) for (s, n) in incs]

        def fn(e):
            for h, v in waits:
                e.wait_ge(h, v)
            inst = body(e) if body is not None else None
            if incs_:
                if inst is None:
                    assert body is None, "completion incs need a real instruction"
                    inst = e.nop()
                for h, n in incs_:
                    inst.then_inc(h, n)

        P.append((eng, fn))

    with contextlib.ExitStack() as st:
        sb = lambda name, shape, dt: st.enter_context(nc.sbuf_tensor(name, shape, dt))

        ones = sb("ones", [128, 128], F32)
        bgs = sb("bgs", [128, KD], F32)
        bvs = sb("bvs", [128, KD], F32)
        bds = sb("bds", [128, KD], F32)
        b1s = sb("b1s", [128, KH], F32)
        b2s = sb("b2s", [128, KD], F32)
        bq16 = sb("bq16", [128, KD], F16)
        bq32 = sb("bq32", [128, KD], F32)
        wmixs = sb("wmixs", [128, KD], F32)
        wffns = sb("wffns", [128, KD], F32)
        selps = sb("selps", [128, 1], F32)
        carry = sb("carry", [128, KD], F32)
        carry2 = sb("carry2", [128, KD], F32)
        hprev = sb("hprev", [128, KD], F32)

        assert TC1 == 2 * TC3
        x_sb = sb("x_sb", [128, KD, TC1], F32)
        xn_sb = sb("xn_sb", [128, KD, TC1], F32)
        pma = sb("pma", [128, KF, TC3], F32)
        xs_sb = sb("xs_sb", [128, KD, TC1], F32)
        a_sb = sb("a_sb", [128, KD, TC1], F32)
        hq_sb = sb("hq_sb", [128, KH, TC1], FP8)
        sg_sb = sb("sg_sb", [128, TC1], F32)
        tv_sb = sb("tv_sb", [128, TC1], F32)
        sl_sb = sb("sl_sb", [128, TC1], F32)
        f16_sb = sb("f16_sb", [128, TC1], F16)
        stat = sb("stat", [1, 6 * TC1], F32)
        wbufA = sb("wbufA", [128, 24, 128], F32)
        wbufB = sb("wbufB", [128, 24, 128], F32)
        wq_sb = sb("wq_sb", [128, KH, 128], FP8)
        p3_xn = sb("p3_xn", [128, KD, TC3], F32)
        p3_xs = sb("p3_xs", [128, KD, TC3], F32)
        p3_a = sb("p3_a", [128, KD, TC3], F32)
        p3_pa = sb("p3_pa", [128, KD, TC3], F32)
        p3_s = sb("p3_s", [128, KD, TC3], F32)
        p3_xb = sb("p3_xb", [128, KD, TC3], F32)
        wobufA = sb("wobufA", [128, 768], F32)
        wobufB = sb("wobufB", [128, 768], F32)

        def xav(kc):
            return pma[:, 2 * kc:2 * kc + 2, :].rearrange("p a b -> p (a b)")

        pb = [st.enter_context(nc.psum_tensor(f"pb{i}", [128, 512], F32))
              for i in range(8)]

        sBIAS = Sem(nc, st, "sBIAS")
        sX = Sem(nc, st, "sX")
        sWA = Sem(nc, st, "sWA")
        sWB = Sem(nc, st, "sWB")
        sP3L = Sem(nc, st, "sP3L")
        sST = Sem(nc, st, "sST")
        sGD = Sem(nc, st, "sGD")
        sT = Sem(nc, st, "sT")
        sV = Sem(nc, st, "sV")
        sS = Sem(nc, st, "sS")
        sCC = Sem(nc, st, "sCC")
        sBAR = Sem(nc, st, "sBAR")
        chain = {n: Sem(nc, st, f"ch_{n[:3]}") for n in ("vector", "scalar")}

        def CI(en, inst):
            s = chain[en]
            s.v += 1
            inst.then_inc(s.h, 1)
            return inst

        def CW(en, e):
            s = chain[en]
            e.wait_ge(s.h, s.v)

        def row(i, n=TC1):
            return stat[0:1, i * TC1: i * TC1 + n]

        step("vector", lambda e: e.memset(ones[:, :], 1.0), incs=[(sV, 1)])
        t_ones = (sV, sV.bump())
        step("vector", lambda e: e.memset(carry[:, :], 0.0), incs=[(sV, 1)])
        t_car0 = (sV, sV.bump())

        for dst, src in ((bgs, bg2), (bvs, bv2), (bds, bd2), (b1s, b12),
                         (b2s, b22), (wmixs, wmix2), (wffns, wffn2), (selps, selp)):
            step("sync", lambda e, d=dst, s=src: e.dma_start(out=d[:, :], in_=s),
                 incs=[(sBIAS, 16)])
            sBIAS.bump(16)
        t_bias = (sBIAS, sBIAS.v)

        def mk_bq(e):
            CI("scalar", e.activation(bq16[:, :], b2s[:, :], AF.Copy))
            CW("scalar", e)
            return e.activation(bq32[:, :], bq16[:, :], AF.Copy)
        step("scalar", mk_bq, waits=[t_bias], incs=[(sS, 1)])
        t_bq = (sS, sS.bump())

        for c in range(NCH1):
            cw = [(sBAR, sBAR.v)] if c > 0 else []
            c0, c1 = c * TC1, (c + 1) * TC1

            step("sync", lambda e, a=c0, b=c1: e.dma_start(
                out=x_sb[:, :, :], in_=xTv[:, :, a:b]), waits=cw, incs=[(sX, 16)])
            t_x = (sX, sX.bump(16))

            step("tensor", lambda e: [e.matmul(pb[4][0:1, 0:TC1], ones[:, 0:1],
                                               x_sb[:, kc, :], start=(kc == 0),
                                               stop=(kc == KD - 1))
                                      for kc in range(KD)][-1],
                 waits=[t_x, t_ones] + cw, incs=[(sT, 1)])
            t_sumx = (sT, sT.bump())
            t_sq = []
            for kc in range(KD):
                scr = sg_sb if kc % 2 == 0 else tv_sb
                w_ = [t_x] + cw if kc < 2 else [(sT, t_sq[kc - 2][1])]
                step("scalar", lambda e, kc=kc, scr=scr: e.activation(
                    scr[:, :], x_sb[:, kc, :], AF.Square), waits=w_, incs=[(sS, 1)])
                t_s = (sS, sS.bump())
                step("tensor", lambda e, kc=kc, scr=scr: e.matmul(
                    pb[5][0:1, 0:TC1], ones[:, 0:1], scr[:, :],
                    start=(kc == 0), stop=(kc == KD - 1)),
                     waits=[t_s], incs=[(sT, 1)])
                t_sq.append((sT, sT.bump()))
            t_sumsq = t_sq[-1]

            def evrows(e):
                CI("scalar", e.copy(row(0), pb[4][0:1, 0:TC1]))
                return e.copy(row(1), pb[5][0:1, 0:TC1])
            step("scalar", evrows, waits=[t_sumx, t_sumsq], incs=[(sS, 1)])
            t_ev = (sS, sS.bump())

            def rows1(e):
                ops = [
                    lambda: e.tensor_scalar(row(0), row(0), INV_D, None, OP.mult),
                    lambda: e.tensor_scalar(row(1), row(1), INV_D, None, OP.mult),
                    lambda: e.tensor_tensor(row(2), row(0), row(0), OP.mult),
                    lambda: e.tensor_tensor(row(1), row(1), row(2), OP.subtract),
                    lambda: e.tensor_scalar(row(2), row(1), 1e-5, None, OP.add),
                    lambda: e.tensor_scalar(row(3), row(2), 1e-6, None, OP.mult),
                ]
                for op in ops:
                    CI("vector", op())
                    CW("vector", e)
                return e.tensor_tensor(row(3), row(1), row(3), OP.add)
            step("vector", rows1, waits=[t_ev], incs=[(sV, 1)])
            t_r1 = (sV, sV.bump())

            def sqrts(e):
                CI("scalar", e.activation(row(1), row(2), AF.Sqrt))
                return e.activation(row(4), row(3), AF.Sqrt)
            step("scalar", sqrts, waits=[t_r1], incs=[(sS, 1)])
            t_sqr = (sS, sS.bump())

            def rows2(e):
                ops = [
                    lambda: e.reciprocal(row(5), row(1)),
                    lambda: e.tensor_tensor(row(5), row(2), row(5), OP.mult),
                    lambda: e.tensor_tensor(row(5), row(1), row(5), OP.add),
                    lambda: e.tensor_scalar(row(5), row(5), 0.5, None, OP.mult),
                    lambda: e.reciprocal(row(1), row(5)),
                    lambda: e.reciprocal(row(5), row(4)),
                    lambda: e.tensor_tensor(row(5), row(3), row(5), OP.mult),
                    lambda: e.tensor_tensor(row(5), row(4), row(5), OP.add),
                    lambda: e.tensor_scalar(row(5), row(5), 0.5, None, OP.mult),
                    lambda: e.reciprocal(row(4), row(5)),
                    lambda: e.tensor_tensor(row(0), row(0), row(1), OP.mult),
                    lambda: e.tensor_tensor(row(5), row(2), row(1), OP.mult),
                ]
                for op in ops:
                    CI("vector", op())
                    CW("vector", e)
                return e.tensor_tensor(row(5), row(5), row(4), OP.mult)
            step("vector", rows2, waits=[t_sqr], incs=[(sV, 1)])
            t_r2 = (sV, sV.bump())

            step("tensor", lambda e: (e.matmul(pb[6][:, 0:TC1], ones[0:1, :], row(1),
                                               start=True, stop=True),
                                      e.matmul(pb[7][:, 0:TC1], ones[0:1, :], row(0),
                                               start=True, stop=True))[-1],
                 waits=[t_r2], incs=[(sT, 1)])
            t_bc1 = (sT, sT.bump())

            def mk_xn(e):
                for kc in range(KD):
                    inst = e.tensor_tensor(xn_sb[:, kc, :], x_sb[:, kc, :],
                                           pb[6][:, 0:TC1], OP.mult)
                CI("vector", inst)
                CW("vector", e)
                for kc in range(KD):
                    inst = e.tensor_tensor(xn_sb[:, kc, :], xn_sb[:, kc, :],
                                           pb[7][:, 0:TC1], OP.subtract)
                return inst
            step("vector", mk_xn, waits=[t_bc1, t_x], incs=[(sV, 1)])
            t_xn = (sV, sV.bump())
            step("scalar", lambda e, a=c0, b=c1: e.dma_start(
                out=xnv[:, :, a:b], in_=xn_sb[:, :, :]), waits=[t_xn], incs=[(sST, 16)])
            sST.bump(16)

            step("tensor", lambda e: e.matmul(pb[6][:, 0:TC1], ones[0:1, :], row(5),
                                              start=True, stop=True),
                 waits=[t_xn], incs=[(sT, 1)])
            t_bc2 = (sT, sT.bump())

            def mk_xa(e):
                for kc in range(KD):
                    inst = e.tensor_tensor(xav(kc), xn_sb[:, kc, :],
                                           pb[6][:, 0:TC1], OP.mult)
                CI("vector", inst)
                CW("vector", e)
                for kc in range(KD):
                    inst = e.tensor_scalar(xav(kc), xav(kc),
                                           wmixs[:, kc:kc + 1], None, OP.mult)
                return inst
            step("vector", mk_xa, waits=[t_bc2], incs=[(sV, 1)])
            t_xa = (sV, sV.bump())

            t_evac = {0: None, 1: None, 2: None}
            t_gvd_done = {}
            t_xs_prev = None
            t_ls_prev = None
            for m in range(KD):
                wb = wbufA if m % 2 == 0 else wbufB
                sw_ = sWA if m % 2 == 0 else sWB
                wwait = ([(sT, t_gvd_done[m - 2][1])] if m >= 2 else []) + cw

                def ldw(e, m=m, wb=wb, sw_=sw_):
                    e.dma_start(out=wb[:, 0:KD, :],
                                in_=wgv[:, :, m * 128:(m + 1) * 128]).then_inc(sw_.h, 16)
                    e.dma_start(out=wb[:, KD:2 * KD, :],
                                in_=wvv[:, :, m * 128:(m + 1) * 128]).then_inc(sw_.h, 16)
                    return e.dma_start(out=wb[:, 2 * KD:3 * KD, :],
                                       in_=wdv[:, :, m * 128:(m + 1) * 128]).then_inc(sw_.h, 16)
                step("sync", ldw, waits=wwait)
                t_w = (sw_, sw_.bump(48))

                for pi in range(3):
                    mw = [t_w, t_xa]
                    if t_evac[pi] is not None:
                        mw.append(t_evac[pi])

                    def mmgvd(e, wb=wb, pi=pi):
                        for kc in range(KD):
                            inst = e.matmul(pb[pi][:, 0:TC1], wb[:, pi * KD + kc, :],
                                            xav(kc), start=(kc == 0),
                                            stop=(kc == KD - 1))
                        return inst
                    step("tensor", mmgvd, waits=mw, incs=[(sT, 1)])
                    t_mm = (sT, sT.bump())
                    if pi == 2:
                        t_gvd_done[m] = t_mm
                    if pi == 0:
                        ew = [t_mm] + ([t_xs_prev] if t_xs_prev else [])
                        step("scalar", lambda e, m=m: e.activation(
                            sg_sb[:, :], pb[0][:, 0:TC1], AF.Sigmoid,
                            bias=bgs[:, m:m + 1]), waits=ew, incs=[(sS, 1)])
                    elif pi == 1:
                        ew = [t_mm] + ([t_xs_prev] if t_xs_prev else [])
                        step("scalar", lambda e, m=m: e.activation(
                            tv_sb[:, :], pb[1][:, 0:TC1], AF.Tanh,
                            bias=bvs[:, m:m + 1]), waits=ew, incs=[(sS, 1)])
                    else:
                        ew = [t_mm] + ([t_ls_prev] if t_ls_prev else [])
                        step("scalar", lambda e, m=m: e.activation(
                            sl_sb[:, :], pb[2][:, 0:TC1], AF.Sigmoid,
                            bias=bds[:, m:m + 1]), waits=ew, incs=[(sS, 1)])
                    t_evac[pi] = (sS, sS.bump())

                step("vector", lambda e, m=m: e.tensor_tensor(
                    xs_sb[:, m, :], sg_sb[:, :], tv_sb[:, :], OP.mult),
                     waits=[t_evac[0], t_evac[1]], incs=[(sV, 1)])
                t_xs = (sV, sV.bump())
                t_xs_prev = t_xs
                step("vector", lambda e, m=m: e.tensor_scalar(
                    a_sb[:, m, :], sl_sb[:, :], 0.998, 0.001, OP.mult, OP.add),
                     waits=[t_evac[2]], incs=[(sV, 1)])
                t_a = (sV, sV.bump())

                cwait = [t_car0] if c == 0 else []

                def lscan(e, m=m):
                    CW("vector", e)
                    CI("vector", e.tensor_tensor_scan(
                        sl_sb[:, :], a_sb[:, m, :], xs_sb[:, m, :],
                        carry[:, m:m + 1], OP.mult, OP.add))
                    CW("vector", e)
                    return e.tensor_copy(carry[:, m:m + 1], sl_sb[:, TC1 - 1:TC1])
                step("vector", lscan, waits=[t_xs, t_a, t_evac[2]] + cwait,
                     incs=[(sV, 1)])
                t_ls_prev = (sV, sV.bump())
            t_gvd_all = (sV, sV.v)
            step("scalar", lambda e, a=c0, b=c1: e.dma_start(
                out=xsv[:, :, a:b], in_=xs_sb[:, :, :]), waits=[t_gvd_all],
                 incs=[(sST, 16)])
            step("scalar", lambda e, a=c0, b=c1: e.dma_start(
                out=av[:, :, a:b], in_=a_sb[:, :, :]), waits=[t_gvd_all],
                 incs=[(sST, 16)])
            sST.bump(32)

            t_h_prev = None
            t_w1_mm = {}
            for m in range(KH):
                wb = wbufA if m % 2 == 0 else wbufB
                sw_ = sWA if m % 2 == 0 else sWB
                wwait = [t_w1_mm[m - 2]] if m >= 2 else [(sT, t_gvd_done[KD - 1][1])]
                step("sync", lambda e, m=m, wb=wb, sw_=sw_: e.dma_start(
                    out=wb[:, 0:KD, :],
                    in_=w1v[:, :, m * 128:(m + 1) * 128]).then_inc(sw_.h, 16),
                     waits=wwait)
                t_w = (sw_, sw_.bump(16))
                mw = [t_w, t_xn]
                if t_h_prev is not None:
                    mw.append(t_h_prev)

                def mmw1(e, wb=wb):
                    for kc in range(KD):
                        inst = e.matmul(pb[3][:, 0:TC1], wb[:, kc, :], xn_sb[:, kc, :],
                                        start=(kc == 0), stop=(kc == KD - 1))
                    return inst
                step("tensor", mmw1, waits=mw, incs=[(sT, 1)])
                t_mm = (sT, sT.bump())
                t_w1_mm[m] = t_mm
                step("scalar", lambda e, m=m: e.activation(
                    hq_sb[:, m, :], pb[3][:, 0:TC1], AF.Relu, bias=b1s[:, m:m + 1]),
                     waits=[t_mm], incs=[(sS, 1)])
                t_h_prev = (sS, sS.bump())
            t_hq_all = t_h_prev

            t_f_prev = None
            t_w2_mm = {}
            t_cast_prev = None
            t_part = None
            for mo in range(KD):
                wwaitA = [t_w2_mm[mo - 1]] if mo >= 1 else [t_w1_mm[KH - 1]]

                def ldw2(e, mo=mo):
                    e.dma_start(out=wbufA[:, 0:16, :],
                                in_=w2v[:, 0:16, mo * 128:(mo + 1) * 128]).then_inc(sWA.h, 16)
                    return e.dma_start(out=wbufB[:, 0:16, :],
                                       in_=w2v[:, 16:32, mo * 128:(mo + 1) * 128]).then_inc(sWB.h, 16)
                step("sync", ldw2,
                     waits=wwaitA + ([t_cast_prev] if t_cast_prev else []))
                t_wA = (sWA, sWA.bump(16))
                t_wB = (sWB, sWB.bump(16))
                cwt = [t_wA, t_wB]
                if t_w2_mm.get(mo - 1):
                    cwt.append(t_w2_mm[mo - 1])

                def castw(e):
                    e.tensor_copy(wq_sb[:, 0:16, :], wbufA[:, 0:16, :])
                    return e.tensor_copy(wq_sb[:, 16:32, :], wbufB[:, 0:16, :])
                step("vector", castw, waits=cwt, incs=[(sV, 1)])
                t_cast_prev = (sV, sV.bump())
                mw = [t_cast_prev, t_hq_all]
                if t_f_prev is not None:
                    mw.append(t_f_prev)

                def mmw2(e):
                    for kh in range(KH):
                        inst = e.matmul(pb[3][:, 0:TC1], wq_sb[:, kh, :],
                                        hq_sb[:, kh, :], start=(kh == 0),
                                        stop=(kh == KH - 1))
                    return inst
                step("tensor", mmw2, waits=mw, incs=[(sT, 1)])
                t_mm = (sT, sT.bump())
                t_w2_mm[mo] = t_mm
                fw = [t_mm, t_bq]
                if t_part is not None:
                    fw.append(t_part)
                step("scalar", lambda e, mo=mo: e.activation(
                    f16_sb[:, :], pb[3][:, 0:TC1], AF.Identity,
                    bias=bq32[:, mo:mo + 1]), waits=fw, incs=[(sS, 1)])
                t_f = (sS, sS.bump())
                t_f_prev = t_f

                def mkpart(e, mo=mo):
                    CI("vector", e.tensor_copy(sl_sb[:, :], f16_sb[:, :]))
                    CW("vector", e)
                    return e.tensor_tensor(x_sb[:, mo, :], x_sb[:, mo, :],
                                           sl_sb[:, :], OP.add)
                step("vector", mkpart, waits=[t_f, t_gvd_all], incs=[(sV, 1)])
                t_part = (sV, sV.bump())
            step("scalar", lambda e, a=c0, b=c1: e.dma_start(
                out=outv[:, :, a:b], in_=x_sb[:, :, :]), waits=[t_part],
                 incs=[(sST, 16)])
            sST.bump(16)

            step("gpsimd", None,
                 waits=[(sT, sT.v), (sV, sV.v), (sS, sS.v), (sST, sST.v),
                        (sX, sX.v), (sWA, sWA.v), (sWB, sWB.v), (sP3L, sP3L.v)],
                 incs=[(sBAR, 1)])
            sBAR.bump()

        step("gpsimd", lambda e: e.dma_start(out=cc_in.ap(), in_=carry[:, :]),
             waits=[(sBAR, sBAR.v)], incs=[(sGD, 16)])
        t_ccin = (sGD, sGD.bump(16))
        step("gpsimd", lambda e: e.collective_compute(
            "AllGather", OP.bypass,
            replica_groups=[[0, 1], [2, 3], [4, 5], [6, 7]],
            ins=[cc_in.ap()], outs=[cc_out.ap()]),
             waits=[t_ccin], incs=[(sCC, 1)])
        t_cc = (sCC, sCC.bump())
        step("gpsimd", lambda e: e.dma_start(
            out=hprev[:, :], in_=cc_out.ap()[0, :, :]), waits=[t_cc], incs=[(sGD, 16)])
        t_hp = (sGD, sGD.bump(16))
        step("vector", lambda e: e.tensor_scalar(
            carry2[:, :], hprev[:, :], selps[:, 0:1], None, OP.mult),
             waits=[t_hp], incs=[(sV, 1)])
        t_c2 = (sV, sV.bump())

        for c in range(NCH3):
            cw = [(sBAR, sBAR.v)]
            c0, c1 = c * TC3, (c + 1) * TC3
            for dst, src in ((p3_xn, xnv), (p3_xs, xsv), (p3_a, av), (p3_pa, outv)):
                step("sync", lambda e, d=dst, s=src, a=c0, b=c1: e.dma_start(
                    out=d[:, :, :], in_=s[:, :, a:b]), waits=cw, incs=[(sP3L, 16)])
                sP3L.bump(16)
            t_ld = (sP3L, sP3L.v)

            def scans(e):
                for m in range(KD):
                    CW("vector", e)
                    CI("vector", e.tensor_tensor_scan(
                        p3_s[:, m, :], p3_a[:, m, :], p3_xs[:, m, :],
                        carry2[:, m:m + 1], OP.mult, OP.add))
                    CW("vector", e)
                    inst = e.tensor_copy(carry2[:, m:m + 1],
                                         p3_s[:, m, TC3 - 1:TC3])
                    if m < KD - 1:
                        CI("vector", inst)
                return inst
            step("vector", scans, waits=[t_ld, t_c2] + cw, incs=[(sV, 1)])
            t_s = (sV, sV.bump())

            def mkxr(e):
                CW("vector", e)
                CI("vector", e.tensor_tensor(p3_s[:, :, :], p3_xn[:, :, :],
                                             p3_s[:, :, :], OP.add))
                CW("vector", e)
                return e.tensor_tensor(p3_pa[:, :, :], p3_pa[:, :, :],
                                       p3_s[:, :, :], OP.add)
            step("vector", mkxr, waits=[t_s], incs=[(sV, 1)])
            t_xr = (sV, sV.bump())

            t_sq3 = []
            for kc in range(KD):
                scr = sg_sb if kc % 2 == 0 else tv_sb
                w_ = [t_xr] + cw if kc < 2 else [(sT, t_sq3[kc - 2][1])]
                step("scalar", lambda e, kc=kc, scr=scr: e.activation(
                    scr[:, 0:TC3], p3_s[:, kc, :], AF.Square), waits=w_, incs=[(sS, 1)])
                t_s3 = (sS, sS.bump())
                step("tensor", lambda e, kc=kc, scr=scr: e.matmul(
                    pb[2][0:1, 0:TC3], ones[:, 0:1], scr[:, 0:TC3],
                    start=(kc == 0), stop=(kc == KD - 1)), waits=[t_s3] + cw,
                     incs=[(sT, 1)])
                t_sq3.append((sT, sT.bump()))
            step("scalar", lambda e: e.copy(row(0, TC3), pb[2][0:1, 0:TC3]),
                 waits=[t_sq3[-1]], incs=[(sS, 1)])
            t_ev3 = (sS, sS.bump())
            step("vector", lambda e: e.tensor_scalar(
                row(1, TC3), row(0, TC3), INV_D, 1e-6, OP.mult, OP.add),
                 waits=[t_ev3], incs=[(sV, 1)])
            t_u3 = (sV, sV.bump())
            step("scalar", lambda e: e.activation(row(2, TC3), row(1, TC3), AF.Sqrt),
                 waits=[t_u3], incs=[(sS, 1)])
            t_y3 = (sS, sS.bump())

            def rows3(e):
                ops = [
                    lambda: e.reciprocal(row(3, TC3), row(2, TC3)),
                    lambda: e.tensor_tensor(row(3, TC3), row(1, TC3), row(3, TC3),
                                            OP.mult),
                    lambda: e.tensor_tensor(row(3, TC3), row(2, TC3), row(3, TC3),
                                            OP.add),
                    lambda: e.tensor_scalar(row(3, TC3), row(3, TC3), 0.5, None,
                                            OP.mult),
                ]
                for op in ops:
                    CI("vector", op())
                    CW("vector", e)
                return e.reciprocal(row(4, TC3), row(3, TC3))
            step("vector", rows3, waits=[t_y3], incs=[(sV, 1)])
            t_r3 = (sV, sV.bump())
            step("tensor", lambda e: e.matmul(pb[3][:, 0:TC3], ones[0:1, :],
                                              row(4, TC3), start=True, stop=True),
                 waits=[t_r3], incs=[(sT, 1)])
            t_bc3 = (sT, sT.bump())

            def mkxb(e):
                for kc in range(KD):
                    inst = e.tensor_tensor(p3_xb[:, kc, :], p3_s[:, kc, :],
                                           pb[3][:, 0:TC3], OP.mult)
                CI("vector", inst)
                CW("vector", e)
                for kc in range(KD):
                    inst = e.tensor_scalar(p3_xb[:, kc, :], p3_xb[:, kc, :],
                                           wffns[:, kc:kc + 1], None, OP.mult)
                return inst
            step("vector", mkxb, waits=[t_bc3], incs=[(sV, 1)])
            t_xb = (sV, sV.bump())

            t_sil_prev = None
            t_pm_prev = None
            t_wo_prev = None
            t_wo_m = {}
            for m in range(KF):
                wb = wbufA if m % 2 == 0 else wbufB
                wob = wobufA if m % 2 == 0 else wobufB
                sw_ = sWA if m % 2 == 0 else sWB
                wwait = ([t_wo_m[m - 2]] if m >= 2 else []) + cw

                def ldgu(e, m=m, wb=wb, wob=wob, sw_=sw_):
                    e.dma_start(out=wb[:, 0:KD, :],
                                in_=wgatev[:, :, m * 128:(m + 1) * 128]).then_inc(sw_.h, 16)
                    e.dma_start(out=wb[:, KD:2 * KD, :],
                                in_=wupv[:, :, m * 128:(m + 1) * 128]).then_inc(sw_.h, 16)
                    return e.dma_start(out=wob[:, :],
                                       in_=woutv[m, :, 0:768]).then_inc(sw_.h, 16)
                step("sync", ldgu, waits=wwait)
                t_w = (sw_, sw_.bump(48))

                mw = [t_w, t_xb]
                if t_sil_prev is not None:
                    mw.append(t_sil_prev)

                def mmgate(e, wb=wb):
                    for kc in range(KD):
                        inst = e.matmul(pb[0][:, 0:TC3], wb[:, kc, :], p3_xb[:, kc, :],
                                        start=(kc == 0), stop=(kc == KD - 1))
                    return inst
                step("tensor", mmgate, waits=mw, incs=[(sT, 1)])
                t_g = (sT, sT.bump())

                mw2 = [t_w, t_xb]
                if t_pm_prev is not None:
                    mw2.append(t_pm_prev)

                def mmup(e, wb=wb):
                    for kc in range(KD):
                        inst = e.matmul(pb[1][:, 0:TC3], wb[:, KD + kc, :],
                                        p3_xb[:, kc, :], start=(kc == 0),
                                        stop=(kc == KD - 1))
                    return inst
                step("tensor", mmup, waits=mw2, incs=[(sT, 1)])
                t_u = (sT, sT.bump())

                sw2 = [t_g] + ([t_pm_prev] if t_pm_prev else [])
                step("scalar", lambda e: e.activation(sg_sb[:, 0:TC3],
                                                      pb[0][:, 0:TC3], AF.Silu),
                     waits=sw2, incs=[(sS, 1)])
                t_sil = (sS, sS.bump())
                t_sil_prev = t_sil

                step("vector", lambda e, m=m: e.tensor_tensor(
                    pma[:, m, :], sg_sb[:, 0:TC3], pb[1][:, 0:TC3], OP.mult),
                     waits=[t_sil, t_u], incs=[(sV, 1)])
                t_pm = (sV, sV.bump())
                t_pm_prev = t_pm

                def mmwo(e, wob=wob, m=m):
                    for mo in range(6):
                        inst = e.matmul(pb[2 + mo][:, 0:TC3],
                                        wob[:, mo * 128:(mo + 1) * 128], pma[:, m, :],
                                        start=(m == 0), stop=(m == KF - 1))
                    return inst
                step("tensor", mmwo, waits=[t_pm, t_w], incs=[(sT, 1)])
                t_wo_prev = (sT, sT.bump())
                t_wo_m[m] = t_wo_prev

            def finev1(e):
                for mo in range(6):
                    inst = e.tensor_tensor(p3_xn[:, mo, :], p3_pa[:, mo, :],
                                           pb[2 + mo][:, 0:TC3], OP.add)
                return inst
            step("vector", finev1, waits=[t_wo_prev, t_xr], incs=[(sV, 1)])
            t_fin1 = (sV, sV.bump())

            t_sw2 = {}
            for m in range(KF):
                wob = wobufA if m % 2 == 0 else wobufB
                sw_ = sWA if m % 2 == 0 else sWB
                gw = [t_sw2[m - 2]] if m >= 2 else [t_wo_m[KF - 2 + m]]
                step("sync", lambda e, m=m, wob=wob, sw_=sw_: e.dma_start(
                    out=wob[:, 0:256], in_=woutv[m, :, 768:1024]).then_inc(sw_.h, 16),
                     waits=gw)
                t_w2 = (sw_, sw_.bump(16))
                mw3 = [t_w2]
                if m == 0:
                    mw3 += [t_pm_prev, t_sil_prev]

                def mm2(e, wob=wob, m=m):
                    e.matmul(pb[0][:, 0:TC3], wob[:, 0:128], pma[:, m, :],
                             start=(m == 0), stop=(m == KF - 1))
                    return e.matmul(pb[1][:, 0:TC3], wob[:, 128:256], pma[:, m, :],
                                    start=(m == 0), stop=(m == KF - 1))
                step("tensor", mm2, waits=mw3, incs=[(sT, 1)])
                t_sw2[m] = (sT, sT.bump())

            def finev2(e):
                e.tensor_tensor(p3_xn[:, 6, :], p3_pa[:, 6, :],
                                pb[0][:, 0:TC3], OP.add)
                return e.tensor_tensor(p3_xn[:, 7, :], p3_pa[:, 7, :],
                                       pb[1][:, 0:TC3], OP.add)
            step("vector", finev2, waits=[t_sw2[KF - 1]], incs=[(sV, 1)])
            t_fin2 = (sV, sV.bump())
            step("scalar", lambda e, a=c0, b=c1: e.dma_start(
                out=outv[:, :, a:b], in_=p3_xn[:, :, :]), waits=[t_fin1, t_fin2],
                 incs=[(sST, 16)])
            sST.bump(16)
            step("gpsimd", None,
                 waits=[(sT, sT.v), (sV, sV.v), (sS, sS.v), (sST, sST.v),
                        (sX, sX.v), (sWA, sWA.v), (sWB, sWB.v), (sP3L, sP3L.v)],
                 incs=[(sBAR, 1)])
            sBAR.bump()

        step("gpsimd", None, waits=[(sST, sST.v)])
        step("sync", None, waits=[(sST, sST.v), (sBAR, sBAR.v)])

        with nc.Block() as blk:
            for eng_name in ("sync", "scalar", "vector", "tensor", "gpsimd"):
                def make(eng_name):
                    def emit(e):
                        for en, fn in P:
                            if en == eng_name:
                                fn(e)
                    return emit
                getattr(blk, eng_name)(make(eng_name))

    return nc


def kernel(L0_attn_x, L0_attn_rms_mix_w, L0_attn_rms_ffn_w, L0_attn_Wg, L0_attn_bg,
           L0_attn_Wv, L0_attn_bv, L0_attn_Wd, L0_attn_bd, L0_attn_W_gate,
           L0_attn_W_up, L0_attn_W_out, L0_ffn_w1, L0_ffn_b1, L0_ffn_w2, L0_ffn_b2):
    x = np.asarray(L0_attn_x, dtype=np.float32)

    def t(a):
        return np.ascontiguousarray(np.asarray(a, dtype=np.float32).T)

    def b2(a, k):
        return np.ascontiguousarray(np.asarray(a, dtype=np.float32).reshape(k, 128).T)

    shared = {
        "wgT": t(L0_attn_Wg), "wvT": t(L0_attn_Wv), "wdT": t(L0_attn_Wd),
        "wgateT": t(L0_attn_W_gate), "wupT": t(L0_attn_W_up), "woutT": t(L0_attn_W_out),
        "w1T": t(L0_ffn_w1), "w2T": t(L0_ffn_w2),
        "bg2": b2(L0_attn_bg, KD), "bv2": b2(L0_attn_bv, KD), "bd2": b2(L0_attn_bd, KD),
        "b12": b2(L0_ffn_b1, KH), "b22": b2(L0_ffn_b2, KD),
        "wmix2": b2(L0_attn_rms_mix_w, KD), "wffn2": b2(L0_attn_rms_ffn_w, KD),
    }
    in_maps = []
    for core in range(8):
        b, h = core // 2, core % 2
        xc = np.ascontiguousarray(x[b, h * T:(h + 1) * T, :].T)
        d = dict(shared)
        d["xT"] = xc
        d["selp"] = np.full((128, 1), float(h), dtype=np.float32)
        in_maps.append(d)

    nc = build_nc()
    res = run_bass_kernel_spmd(nc, in_maps, core_ids=list(range(8)))
    out = np.empty((B, S, D), dtype=np.float32)
    for core in range(8):
        b, h = core // 2, core % 2
        out[b, h * T:(h + 1) * T, :] = res.results[core]["outT"].T
    return out
